# version 01
# baseline (speedup 1.0000x reference)
import contextlib
import numpy as np
import concourse.bass as bass
import concourse.mybir as mybir
from concourse.bass_utils import run_bass_kernel_spmd

F32 = mybir.dt.float32
BF16 = mybir.dt.bfloat16
I32 = mybir.dt.int32
AF = mybir.ActivationFunctionType
OP = mybir.AluOpType
AX = mybir.AxisListType

D = 1024
T = 4096
H = 4096
NCH = 8
HCH = 32
TT = 512
NT = T // TT
TA = 256
NA = T // TA
CMAGIC = 12582912.0
EPS = 1e-5
QP = 127.0
MINR = 1e-6


class Sem:
    def __init__(self, stack, nc, name):
        self.h = stack.enter_context(nc.semaphore(name))
        self.n = 0

    def inc(self, inst, k=1):
        inst.then_inc(self.h, k)
        self.n += k
        return self.n


def _flat(th, off, n, rowlen):
    return bass.AP(th, off, [[rowlen, 128], [1, n]])


def build_program():
    nc = bass.Bass("TRN2", target_bir_lowering=False, debug=False)

    xT = nc.dram_tensor("xT", [D, T], F32, kind="ExternalInput").ap()
    w1T = nc.dram_tensor("w1T", [D, H], F32, kind="ExternalInput").ap()
    w2T = nc.dram_tensor("w2T", [H, D], F32, kind="ExternalInput").ap()
    winT = nc.dram_tensor("winT", [D, D], F32, kind="ExternalInput").ap()
    waT = nc.dram_tensor("waT", [D, D], F32, kind="ExternalInput").ap()
    wbT = nc.dram_tensor("wbT", [D, D], F32, kind="ExternalInput").ap()
    wvT = nc.dram_tensor("wvT", [D, D], F32, kind="ExternalInput").ap()
    woT = nc.dram_tensor("woT", [D, D], F32, kind="ExternalInput").ap()
    b1p = nc.dram_tensor("b1p", [128, HCH], F32, kind="ExternalInput").ap()
    b2p = nc.dram_tensor("b2p", [128, NCH], F32, kind="ExternalInput").ap()
    bap = nc.dram_tensor("bap", [128, NCH], F32, kind="ExternalInput").ap()
    bbp = nc.dram_tensor("bbp", [128, NCH], F32, kind="ExternalInput").ap()
    bvp = nc.dram_tensor("bvp", [128, NCH], F32, kind="ExternalInput").ap()
    outT = nc.dram_tensor("outT", [D, T], F32, kind="ExternalOutput").ap()

    ln_d = nc.dram_tensor("ln_d", [NCH, 128, T], F32, kind="Internal").ap()
    x1_d = nc.dram_tensor("x1_d", [NCH, 128, T], F32, kind="Internal").ap()
    x2_d = nc.dram_tensor("x2_d", [NCH, 128, T], F32, kind="Internal").ap()
    h_d = nc.dram_tensor("h_d", [NT, HCH, 128, TT], F32, kind="Internal").ap()
    w1q_d = nc.dram_tensor("w1q_d", [128, NCH, H], BF16, kind="Internal").ap()
    w2q_d = nc.dram_tensor("w2q_d", [NCH, 128, HCH, 128], BF16, kind="Internal").ap()
    bounce = nc.dram_tensor("bounce", [1, 128], F32, kind="Internal").ap()
    cc_in = nc.dram_tensor("cc_in", [1, 128], F32, kind="Internal").ap()
    cc_out = nc.dram_tensor("cc_out", [1, 128], F32, kind="Internal").ap()

    xT_r = xT.rearrange("(c p) t -> p c t", p=128)
    w1T_r = w1T.rearrange("(c p) j -> p c j", p=128)
    w2T_r = w2T.rearrange("(a p) d -> p a d", p=128)
    outT_r = outT.rearrange("(c p) t -> p c t", p=128)
    winT_r = winT.rearrange("(c p) e -> p c e", p=128)
    waT_r = waT.rearrange("(c p) e -> p c e", p=128)
    wbT_r = wbT.rearrange("(c p) e -> p c e", p=128)
    wvT_r = wvT.rearrange("(c p) e -> p c e", p=128)
    woT_r = woT.rearrange("(c p) e -> p c e", p=128)

    st = contextlib.ExitStack()

    def sb(name, shape, dt):
        return st.enter_context(nc.sbuf_tensor(name, shape, dt))

    ones_bf = sb("ones_bf", [128, 128], BF16)
    ones_f1 = sb("ones_f1", [1, 128], F32)
    cst = sb("cst", [128, 2], F32)
    EPS_A = cst[:, 0:1]
    b1sb = sb("b1sb", [128, HCH], F32)
    b2sb = sb("b2sb", [128, NCH], F32)
    b1s = sb("b1s", [128, HCH], F32)
    b2s = sb("b2s", [128, NCH], F32)
    bap_s = sb("bap_s", [128, NCH], F32)
    bbp_s = sb("bbp_s", [128, NCH], F32)
    bvp_s = sb("bvp_s", [128, NCH], F32)
    sw1_b = sb("sw1_b", [128, 1], F32)
    sw2_b = sb("sw2_b", [128, 1], F32)
    scl = sb("scl", [128, 12], F32)
    s11 = sb("s11", [1, 2], F32)
    row128 = sb("row128", [1, 128], F32)
    rmax = sb("rmax", [128, 1], F32)
    tmax = sb("tmax", [128, 1], F32)

    pb = [st.enter_context(nc.psum_tensor(f"pb{i}", [128, 512], F32))
          for i in range(8)]

    S = {}
    for name in ["ld", "ld2", "sp", "g", "cc", "pe", "vch",
                 "sq", "mm2", "sd", "mm", "ev",
                 "sq2", "mm3", "sd3",
                 "mi", "xps", "ale", "bee", "vee", "pv", "sg", "mo",
                 "spx2", "spln"]:
        S[name] = Sem(st, nc, "s_" + name)

    VCH = S["vch"]

    def C(ins):
        if VCH.n > 0:
            ins.wait_op(VCH.h, VCH.n, "sem-ge")
        VCH.inc(ins)
        return VCH.n

    def snap():
        return {k: s.n for k, s in S.items()}

    def DMA(eng, sem, out, in_):
        if sem.n > 0:
            eng.wait_ge(sem.h, sem.n)
        return sem.inc(eng.dma_start(out=out, in_=in_), 16)

    with nc.Block("b0") as blk:
        @blk.vector
        def _(v):
            C(v.memset(ones_bf[:], 1.0))
            C(v.memset(ones_f1[:], 1.0))
            C(v.memset(cst[:, 0:1], EPS))
            C(v.memset(rmax[:], 0.0))

        @blk.sync
        def _(s):
            DMA(s, S["ld"], b1sb[:], b1p)
            DMA(s, S["ld"], b2sb[:], b2p)
            DMA(s, S["ld"], bap_s[:], bap)
            DMA(s, S["ld"], bbp_s[:], bbp)
            DMA(s, S["ld"], bvp_s[:], bvp)
            s.wait_ge(S["ld"].h, S["ld"].n)

    b = snap()
    with (
        nc.sbuf_tensor("a_x", [128, 2, NCH, TT], F32) as a_x,
        nc.sbuf_tensor("a_xbf", [128, 2, NCH, TT], BF16) as a_xbf,
        nc.sbuf_tensor("a_sqbf", [128, 2, NCH, TT], BF16) as a_sqbf,
        nc.sbuf_tensor("a_ln", [128, 2, NCH, TT], F32) as a_ln,
        nc.sbuf_tensor("a_st", [128, 2, 4, TT], F32) as a_st,
        nc.sbuf_tensor("a_rstd", [128, 2, TT], F32) as a_rstd,
        nc.Block("b1") as blk,
    ):
        NROW = 2 * NCH * TT
        SROW = 2 * 4 * TT
        RROW = 2 * TT
        VC = {}

        def xf(k):
            return _flat(a_x, k * NCH * TT, NCH * TT, NROW)

        def x3(k):
            return bass.AP(a_x, k * NCH * TT, [[NROW, 128], [TT, NCH], [1, TT]])

        def stt(k, slot):
            return _flat(a_st, k * 4 * TT + slot * TT, TT, SROW)

        def stt_b(k, slot):
            return bass.AP(a_st, k * 4 * TT + slot * TT,
                           [[SROW, 128], [0, NCH], [1, TT]])

        def rstd_b(k):
            return bass.AP(a_rstd, k * TT, [[RROW, 128], [0, NCH], [1, TT]])

        @blk.vector
        def _(v):
            def XBF(i):
                v.wait_ge(S["ld"].h, b["ld"] + 16 * (i + 1))
                if i >= 2:
                    v.wait_ge(S["mm2"].h, b["mm2"] + i - 1)
                VC["xbf", i] = C(v.tensor_copy(
                    out=_flat(a_xbf, (i % 2) * NCH * TT, NCH * TT, NROW),
                    in_=xf(i % 2)))

            def TAIL(i):
                k = i % 2
                pa, pq = pb[2 * k], pb[2 * k + 1]
                v.wait_ge(S["mm2"].h, b["mm2"] + i + 1)
                C(v.tensor_scalar(stt(k, 0), pa[:, 0:TT], 1.0 / D, None, OP.mult))
                C(v.tensor_scalar(stt(k, 1), pq[:, 0:TT], 1.0 / D, None, OP.mult))
                C(v.tensor_mul(stt(k, 3), stt(k, 0), stt(k, 0)))
                VC["var", i] = C(v.tensor_sub(stt(k, 2), stt(k, 1), stt(k, 3)))
                v.wait_ge(S["sd"].h, b["sd"] + i + 1)
                C(v.reciprocal(out=_flat(a_rstd, k * TT, TT, RROW), in_=stt(k, 3)))
                if i >= 2:
                    v.wait_ge(S["sp"].h, b["sp"] + 16 * (i - 1))
                lt = bass.AP(a_ln, k * NCH * TT, [[NROW, 128], [TT, NCH], [1, TT]])
                C(v.tensor_tensor(out=lt, in0=x3(k), in1=stt_b(k, 0),
                                  op=OP.subtract))
                C(v.tensor_tensor(out=lt, in0=lt, in1=rstd_b(k), op=OP.mult))
                C(v.reduce_max(out=tmax[:],
                               in_=_flat(a_ln, k * NCH * TT, NCH * TT, NROW),
                               axis=AX.X, apply_absolute_value=True))
                VC["ln", i] = C(v.tensor_tensor(out=rmax[:], in0=rmax[:],
                                                in1=tmax[:], op=OP.max))

            for i in range(NT):
                XBF(i)
                if i >= 1:
                    TAIL(i - 1)
            TAIL(NT - 1)

        @blk.sync
        def _(s):
            for i in range(NT):
                if i >= 2:
                    s.wait_ge(VCH.h, VC["ln", i - 2])
                    s.wait_ge(S["sq"].h, b["sq"] + i - 1)
                DMA(s, S["ld"], 
                    x3(i % 2), xT_r[:, :, i * TT:(i + 1) * TT])

        @blk.scalar
        def _(sc):
            def SQ(i):
                sc.wait_ge(S["ld"].h, b["ld"] + 16 * (i + 1))
                if i >= 2:
                    sc.wait_ge(S["mm2"].h, b["mm2"] + i - 1)
                S["sq"].inc(sc.activation(
                    out=_flat(a_sqbf, (i % 2) * NCH * TT, NCH * TT, NROW),
                    in_=xf(i % 2), func=AF.Square))

            def SD(i):
                sc.wait_ge(VCH.h, VC["var", i])
                S["sd"].inc(sc.activation(
                    out=stt(i % 2, 3), in_=stt(i % 2, 2),
                    func=AF.Sqrt, bias=EPS_A))

            for i in range(NT):
                SQ(i)
                if i >= 1:
                    SD(i - 1)
            SD(NT - 1)

        @blk.tensor
        def _(t):
            for i in range(NT):
                t.wait_ge(VCH.h, VC["xbf", i])
                t.wait_ge(S["sq"].h, b["sq"] + i + 1)
                if i >= 2:
                    t.wait_ge(VCH.h, VC["var", i - 2])
                pa, pq = pb[2 * (i % 2)], pb[2 * (i % 2) + 1]
                for c in range(NCH):
                    t.matmul(pa[:, 0:TT], ones_bf[:],
                             _flat(a_xbf, (i % 2) * NCH * TT + c * TT, TT, NROW),
                             start=(c == 0), stop=(c == NCH - 1))
                for c in range(NCH):
                    ins = t.matmul(pq[:, 0:TT], ones_bf[:],
                                   _flat(a_sqbf, (i % 2) * NCH * TT + c * TT,
                                         TT, NROW),
                                   start=(c == 0), stop=(c == NCH - 1))
                S["mm2"].inc(ins)

        @blk.gpsimd
        def _(g):
            for i in range(NT):
                g.wait_ge(VCH.h, VC["ln", i])
                dst = ln_d[:, :, i * TT:(i + 1) * TT].transpose([1, 0, 2])
                DMA(g, S["sp"], 
                    dst,
                    bass.AP(a_ln, (i % 2) * NCH * TT,
                                [[NROW, 128], [TT, NCH], [1, TT]]))
            g.wait_ge(VCH.h, VC["ln", NT - 1])
            DMA(g, S["sp"], cc_in[0:1, :].transpose([1, 0]),
                                    rmax[:, 0:1])
            g.wait_ge(S["sp"].h, S["sp"].n)
            S["cc"].inc(g.collective_compute(
                "AllReduce", OP.max, replica_groups=[list(range(8))],
                ins=[cc_in], outs=[cc_out]))

    b = snap()
    with (
        nc.sbuf_tensor("wbuf", [128, 2, 4096], F32) as wbuf,
        nc.sbuf_tensor("wqi", [128, 4096], I32) as wqi,
        nc.sbuf_tensor("wqbf", [128, 2, 4096], BF16) as wqbf,
        nc.sbuf_tensor("invw", [128, 1], F32) as invw,
        nc.Block("b2") as blk,
    ):
        WROW = 2 * 4096
        VC = {}

        def wf(k):
            return _flat(wbuf, k * 4096, 4096, WROW)

        def wqf(k):
            return _flat(wqbf, k * 4096, 4096, WROW)

        @blk.vector
        def _(v):
            C(v.memset(rmax[:], 0.0))
            for c in range(NCH):
                v.wait_ge(S["ld"].h, b["ld"] + 16 * (c + 1))
                C(v.reduce_max(out=tmax[:], in_=wf(c % 2), axis=AX.X,
                               apply_absolute_value=True))
                VC["wred", c] = C(v.tensor_tensor(out=rmax[:], in0=rmax[:],
                                                  in1=tmax[:], op=OP.max))
            v.wait_ge(S["g"].h, b["g"] + 32)
            C(v.reduce_max(out=s11[0:1, 0:1], in_=row128[:], axis=AX.X))
            C(v.tensor_scalar(s11[0:1, 0:1], s11[0:1, 0:1], MINR, None, OP.max))
            VC["s1"] = C(v.tensor_scalar(s11[0:1, 0:1], s11[0:1, 0:1],
                                         1.0 / QP, None, OP.mult))
            v.wait_ge(S["pe"].h, b["pe"] + 1)
            VC["row1"] = C(v.tensor_copy(out=sw1_b[:], in_=pb[0][:, 0:1]))
            C(v.reciprocal(out=invw[:], in_=sw1_b[:]))
            C(v.memset(rmax[:], 0.0))
            for c in range(NCH):
                v.wait_ge(S["ld"].h, b["ld"] + 16 * (NCH + c + 1))
                if c >= 2:
                    v.wait_ge(S["sp"].h, b["sp"] + 16 * (c - 1))
                C(v.tensor_scalar(wf(c % 2), wf(c % 2), invw[:, 0:1], CMAGIC,
                                  OP.mult, OP.add))
                C(v.tensor_scalar(wqi[:], wf(c % 2), CMAGIC, None, OP.subtract))
                VC["wq", c] = C(v.tensor_copy(out=wqf(c % 2), in_=wqi[:]))
            for a in range(8):
                v.wait_ge(S["ld"].h, b["ld"] + 16 * (16 + a + 1))
                C(v.reduce_max(out=tmax[:], in_=wf(a % 2), axis=AX.X,
                               apply_absolute_value=True))
                VC["wred", NCH + a] = C(v.tensor_tensor(
                    out=rmax[:], in0=rmax[:], in1=tmax[:], op=OP.max))
            v.wait_ge(S["g"].h, b["g"] + 64)
            C(v.reduce_max(out=s11[0:1, 0:1], in_=row128[:], axis=AX.X))
            C(v.tensor_scalar(s11[0:1, 0:1], s11[0:1, 0:1], MINR, None, OP.max))
            VC["s2"] = C(v.tensor_scalar(s11[0:1, 0:1], s11[0:1, 0:1],
                                         1.0 / QP, None, OP.mult))
            v.wait_ge(S["pe"].h, b["pe"] + 2)
            VC["row2"] = C(v.tensor_copy(out=sw2_b[:], in_=pb[0][:, 0:1]))
            C(v.reciprocal(out=invw[:], in_=sw2_b[:]))
            for a in range(8):
                v.wait_ge(S["ld"].h, b["ld"] + 16 * (24 + a + 1))
                if a >= 2:
                    v.wait_ge(S["sp"].h, b["sp"] + 16 * NCH + 128 * (a - 1))
                else:
                    v.wait_ge(S["sp"].h, b["sp"] + 16 * (7 + a))
                C(v.tensor_scalar(wf(a % 2), wf(a % 2), invw[:, 0:1], CMAGIC,
                                  OP.mult, OP.add))
                C(v.tensor_scalar(wqi[:], wf(a % 2), CMAGIC, None, OP.subtract))
                VC["wq", NCH + a] = C(v.tensor_copy(out=wqf(a % 2), in_=wqi[:]))

        @blk.sync
        def _(s):
            for c in range(NCH):
                if c >= 2:
                    s.wait_ge(VCH.h, VC["wred", c - 2])
                DMA(s, S["ld"], wf(c % 2), w1T_r[:, c, :])
            for c in range(NCH):
                if c < 2:
                    s.wait_ge(VCH.h, VC["wred", NCH - 2 + c])
                else:
                    s.wait_ge(VCH.h, VC["wq", c - 2])
                DMA(s, S["ld"], wf(c % 2), w1T_r[:, c, :])
            for a in range(8):
                if a < 2:
                    s.wait_ge(VCH.h, VC["wq", NCH - 2 + a])
                else:
                    s.wait_ge(VCH.h, VC["wred", NCH + a - 2])
                DMA(s, S["ld"], 
                    bass.AP(wbuf, (a % 2) * 4096,
                                [[WROW, 128], [1024, 4], [1, 1024]]),
                    w2T_r[:, 4 * a:4 * a + 4, :])
            for a in range(8):
                if a < 2:
                    s.wait_ge(VCH.h, VC["wred", 16 - 2 + a])
                else:
                    s.wait_ge(VCH.h, VC["wq", NCH + a - 2])
                DMA(s, S["ld"], 
                    bass.AP(wbuf, (a % 2) * 4096,
                                [[WROW, 128], [1024, 4], [1, 1024]]),
                    w2T_r[:, 4 * a:4 * a + 4, :])

        @blk.tensor
        def _(t):
            for k, key in ((1, "s1"), (2, "s2")):
                t.wait_ge(VCH.h, VC[key])
                if k == 2:
                    t.wait_ge(VCH.h, VC["row1"])
                S["pe"].inc(t.matmul(pb[0][:, 0:1], ones_f1[:],
                                     s11[0:1, 0:1], start=True, stop=True))

        @blk.gpsimd
        def _(g):
            g.wait_ge(VCH.h, VC["wred", NCH - 1])
            DMA(g, S["g"], bounce[0:1, :].transpose([1, 0]),
                                   rmax[:, 0:1])
            g.wait_ge(S["g"].h, b["g"] + 16)
            DMA(g, S["g"], row128[:], bounce[0:1, :])
            for c in range(NCH):
                g.wait_ge(VCH.h, VC["wq", c])
                DMA(g, S["sp"], w1q_d[:, c, :], wqf(c % 2))
            g.wait_ge(VCH.h, VC["wred", 15])
            g.wait_ge(VCH.h, VC["s1"])
            DMA(g, S["g"], bounce[0:1, :].transpose([1, 0]),
                                   rmax[:, 0:1])
            g.wait_ge(S["g"].h, b["g"] + 48)
            DMA(g, S["g"], row128[:], bounce[0:1, :])
            for a in range(8):
                g.wait_ge(VCH.h, VC["wq", NCH + a])
                for dc in range(NCH):
                    dst = w2q_d[dc, :, 4 * a:4 * a + 4, :]
                    src = bass.AP(wqbf, (a % 2) * 4096 + dc * 128,
                                  [[WROW, 128], [1024, 4], [1, 128]])
                    DMA(g, S["sp"], dst, src)
            g.wait_ge(S["sp"].h, S["sp"].n)

    def scale_finalize_v(v, bph, col, sw_tile, half_, bias_src, bias_dst):
        v.wait_ge(S["g"].h, bph["g"] + 16)
        C(v.reduce_max(out=s11[0:1, 0:1], in_=row128[:], axis=AX.X))
        C(v.tensor_scalar(s11[0:1, 0:1], s11[0:1, 0:1], MINR, None, OP.max))
        n_s = C(v.tensor_scalar(s11[0:1, 0:1], s11[0:1, 0:1], 1.0 / QP, None,
                                OP.mult))
        v.wait_ge(S["pe"].h, bph["pe"] + 1)
        C(v.tensor_copy(out=scl[:, col:col + 1], in_=pb[0][:, 0:1]))
        C(v.reciprocal(out=scl[:, col + 1:col + 2], in_=scl[:, col:col + 1]))
        C(v.tensor_mul(scl[:, col + 2:col + 3], sw_tile[:, 0:1],
                       scl[:, col:col + 1]))
        if half_:
            C(v.tensor_scalar(scl[:, col + 2:col + 3], scl[:, col + 2:col + 3],
                              0.5, None, OP.mult))
        n_done = C(v.tensor_scalar(bias_dst[:], bias_src[:],
                                   scl[:, col + 2:col + 3], None, OP.mult))
        return n_s, n_done

    def emit_passB(ncc, col):
        b = snap()
        sfx = "_%d" % ncc
        with (
            nc.sbuf_tensor("w1q_sb" + sfx, [128, NCH, H], BF16) as w1q_sb,
            nc.sbuf_tensor("lbuf" + sfx, [128, 2, NCH, TT], F32) as lbuf,
            nc.sbuf_tensor("qib" + sfx, [128, NCH, TT], I32) as qib,
            nc.sbuf_tensor("qbf" + sfx, [128, 2, NCH, TT], BF16) as qbf,
            nc.sbuf_tensor("gbuf" + sfx, [128, 2, 4, TT], F32) as gbuf,
            nc.Block("b_ffnB" + str(ncc)) as blk,
        ):
            LROW = 2 * NCH * TT
            GROW = 2 * 4 * TT
            VC = {}

            @blk.vector
            def _(v):
                n_s, n_fin = scale_finalize_v(v, b, col, sw1_b, False, b1sb, b1s)
                VC["s"] = n_s
                VC["fin"] = n_fin
                C(v.memset(rmax[:], 0.0))

                def QNT(i):
                    v.wait_ge(S["ld"].h, b["ld"] + 16 * (i + 2))
                    if i >= 2:
                        v.wait_ge(S["mm"].h, b["mm"] + HCH * (i - 1))
                    lf = _flat(lbuf, (i % 2) * NCH * TT, NCH * TT, LROW)
                    C(v.tensor_scalar(lf, lf, scl[:, col + 1:col + 2], CMAGIC,
                                      OP.mult, OP.add))
                    C(v.tensor_scalar(_flat(qib, 0, NCH * TT, NCH * TT), lf,
                                      CMAGIC, None, OP.subtract))
                    VC["qa", i] = C(v.tensor_copy(
                        out=_flat(qbf, (i % 2) * NCH * TT, NCH * TT, LROW),
                        in_=_flat(qib, 0, NCH * TT, NCH * TT)))

                def HMAX(gidx):
                    v.wait_ge(S["ev"].h, b["ev"] + 4 * (gidx + 1))
                    C(v.reduce_max(out=tmax[:],
                                   in_=_flat(gbuf, (gidx % 2) * 4 * TT,
                                             4 * TT, GROW),
                                   axis=AX.X, apply_absolute_value=True))
                    VC["hm", gidx] = C(v.tensor_tensor(
                        out=rmax[:], in0=rmax[:], in1=tmax[:], op=OP.max))

                QNT(0)
                if NT > 1:
                    QNT(1)
                for gidx in range(8 * NT):
                    HMAX(gidx)
                    if gidx % 8 == 7 and gidx // 8 + 2 < NT:
                        QNT(gidx // 8 + 2)

            @blk.sync
            def _(s):
                DMA(s, S["ld"], w1q_sb[:], w1q_d)
                for i in range(NT):
                    if i >= 2:
                        s.wait_ge(VCH.h, VC["qa", i - 2])
                    src = ln_d[:, :, i * TT:(i + 1) * TT].transpose([1, 0, 2])
                    DMA(s, S["ld"], 
                        bass.AP(lbuf, (i % 2) * NCH * TT,
                                    [[LROW, 128], [TT, NCH], [1, TT]]),
                        src)

            @blk.tensor
            def _(t):
                t.wait_ge(VCH.h, VC["s"])
                S["pe"].inc(t.matmul(pb[0][:, 0:1], ones_f1[:], s11[0:1, 0:1],
                                     start=True, stop=True))
                t.wait_ge(VCH.h, VC["fin"])
                t.wait_ge(S["ld"].h, b["ld"] + 16)
                for i in range(NT):
                    t.wait_ge(VCH.h, VC["qa", i])
                    for j in range(HCH):
                        m = HCH * i + j
                        if m >= 2:
                            t.wait_ge(S["ev"].h, b["ev"] + m - 1)
                        pj = pb[j % 2]
                        for c in range(NCH):
                            ins = t.matmul(
                                pj[:, 0:TT],
                                bass.AP(w1q_sb, c * H + j * 128,
                                        [[NCH * H, 128], [1, 128]]),
                                _flat(qbf, (i % 2) * NCH * TT + c * TT, TT, LROW),
                                start=(c == 0), stop=(c == NCH - 1))
                        S["mm"].inc(ins)

            @blk.scalar
            def _(sc):
                for i in range(NT):
                    for j in range(HCH):
                        m = HCH * i + j
                        sc.wait_ge(S["mm"].h, b["mm"] + m + 1)
                        gidx = m // 4
                        if m % 4 == 0 and gidx >= 2:
                            sc.wait_ge(S["sp"].h, b["sp"] + 16 * (gidx - 1))
                            sc.wait_ge(VCH.h, VC["hm", gidx - 2])
                        S["ev"].inc(sc.activation(
                            out=_flat(gbuf, (gidx % 2) * 4 * TT + (m % 4) * TT,
                                      TT, GROW),
                            in_=pb[j % 2][:, 0:TT], func=AF.Gelu,
                            scale=scl[:, col + 2:col + 3],
                            bias=b1s[:, j:j + 1]))

            @blk.gpsimd
            def _(g):
                g.wait_ge(S["cc"].h, ncc)
                DMA(g, S["g"], row128[:], cc_out[0:1, :])
                for gidx in range(8 * NT):
                    i, gg = gidx // 8, gidx % 8
                    g.wait_ge(S["ev"].h, b["ev"] + 4 * (gidx + 1))
                    dst = h_d[i, 4 * gg:4 * gg + 4].transpose([1, 0, 2])
                    DMA(g, S["sp"], 
                        dst,
                        bass.AP(gbuf, (gidx % 2) * 4 * TT,
                                    [[GROW, 128], [TT, 4], [1, TT]]))
                g.wait_ge(VCH.h, VC["hm", 8 * NT - 1])
                DMA(g, S["sp"], cc_in[0:1, :].transpose([1, 0]),
                                        rmax[:, 0:1])
                g.wait_ge(S["sp"].h, S["sp"].n)
                S["cc"].inc(g.collective_compute(
                    "AllReduce", OP.max, replica_groups=[list(range(8))],
                    ins=[cc_in], outs=[cc_out]))

    def emit_passC(ncc, col, res_src_fn, dst_fn):
        b = snap()
        sfx = "_%d" % ncc
        with (
            nc.sbuf_tensor("hbuf" + sfx, [128, 2, NCH, TT], F32) as hbuf,
            nc.sbuf_tensor("hqi" + sfx, [128, NCH, TT], I32) as hqi,
            nc.sbuf_tensor("hqbf" + sfx, [128, 2, HCH, TT], BF16) as hqbf,
            nc.sbuf_tensor("w2b" + sfx, [128, 2, HCH, 128], BF16) as w2b,
            nc.sbuf_tensor("x0b" + sfx, [128, 2, NCH, TT], F32) as x0b,
            nc.sbuf_tensor("obuf" + sfx, [128, 2, TT], F32) as obuf,
            nc.sbuf_tensor("x1b" + sfx, [128, 2, NCH, TT], F32) as x1b,
            nc.Block("b_ffnC" + str(ncc)) as blk,
        ):
            HROW = 2 * NCH * TT
            QROW = 2 * HCH * TT
            WROW = 2 * HCH * 128
            XROW = 2 * NCH * TT
            OROW = 2 * TT
            VC = {}

            @blk.vector
            def _(v):
                n_s, n_fin = scale_finalize_v(v, b, col, sw2_b, True, b2sb, b2s)
                VC["s"] = n_s
                VC["fin"] = n_fin

                def QNTH(i, gq):
                    m = 4 * i + gq
                    v.wait_ge(S["ld"].h, b["ld"] + 16 * (5 * i + gq + 1))
                    if gq == 0 and i >= 2:
                        v.wait_ge(S["mm"].h, b["mm"] + NCH * (i - 1))
                    hf = _flat(hbuf, (m % 2) * NCH * TT, NCH * TT, HROW)
                    C(v.tensor_scalar(hf, hf, scl[:, col + 1:col + 2], CMAGIC,
                                      OP.mult, OP.add))
                    C(v.tensor_scalar(_flat(hqi, 0, NCH * TT, NCH * TT), hf,
                                      CMAGIC, None, OP.subtract))
                    VC["qa", m] = C(v.tensor_copy(
                        out=_flat(hqbf, (i % 2) * HCH * TT + gq * NCH * TT,
                                  NCH * TT, QROW),
                        in_=_flat(hqi, 0, NCH * TT, NCH * TT)))

                def RES(i):
                    for d in range(NCH):
                        m = NCH * i + d
                        v.wait_ge(S["ev"].h, b["ev"] + m + 1)
                        v.wait_ge(S["ld"].h, b["ld"] + 16 * (5 * i + 5))
                        if d == 0 and i >= 2:
                            v.wait_ge(S["sp"].h, b["sp"] + 16 * (i - 1))
                        VC["res", m] = C(v.tensor_add(
                            _flat(x1b, (i % 2) * NCH * TT + d * TT, TT, XROW),
                            _flat(obuf, (d % 2) * TT, TT, OROW),
                            _flat(x0b, (i % 2) * NCH * TT + d * TT, TT, XROW)))

                for i in range(NT):
                    for gq in range(4):
                        QNTH(i, gq)
                    if i >= 1:
                        RES(i - 1)
                RES(NT - 1)

            @blk.sync
            def _(s):
                for i in range(NT):
                    for gq in range(4):
                        m = 4 * i + gq
                        if m >= 2:
                            s.wait_ge(VCH.h, VC["qa", m - 2])
                        src = h_d[i, 8 * gq:8 * gq + 8].transpose([1, 0, 2])
                        DMA(s, S["ld"], 
                            bass.AP(hbuf, (m % 2) * NCH * TT,
                                        [[HROW, 128], [TT, NCH], [1, TT]]),
                            src)
                    if i >= 2:
                        s.wait_ge(VCH.h, VC["res", NCH * (i - 1) - 1])
                    DMA(s, S["ld"], 
                        bass.AP(x0b, (i % 2) * NCH * TT,
                                    [[XROW, 128], [TT, NCH], [1, TT]]),
                        res_src_fn(i))

            @blk.scalar
            def _(sc):
                def W2S(m):
                    d = m % NCH
                    if m >= 2:
                        sc.wait_ge(S["mm"].h, b["mm"] + m - 1)
                    DMA(sc, S["ld2"],
                        bass.AP(w2b, (d % 2) * HCH * 128,
                                [[WROW, 128], [128, HCH], [1, 128]]),
                        w2q_d[d])

                def OEV(m):
                    d = m % NCH
                    sc.wait_ge(S["mm"].h, b["mm"] + m + 1)
                    if m >= 2:
                        sc.wait_ge(VCH.h, VC["res", m - 2])
                    S["ev"].inc(sc.activation(
                        out=_flat(obuf, (d % 2) * TT, TT, OROW),
                        in_=pb[d % 2][:, 0:TT], func=AF.Identity,
                        scale=scl[:, col + 2:col + 3],
                        bias=b2s[:, d:d + 1]))

                W2S(0)
                W2S(1)
                for m in range(NCH * NT):
                    OEV(m)
                    if m + 2 < NCH * NT:
                        W2S(m + 2)

            @blk.tensor
            def _(t):
                t.wait_ge(VCH.h, VC["s"])
                S["pe"].inc(t.matmul(pb[0][:, 0:1], ones_f1[:], s11[0:1, 0:1],
                                     start=True, stop=True))
                t.wait_ge(VCH.h, VC["fin"])
                for i in range(NT):
                    t.wait_ge(VCH.h, VC["qa", 4 * i + 3])
                    for d in range(NCH):
                        m = NCH * i + d
                        t.wait_ge(S["ld2"].h, b["ld2"] + 16 * (m + 1))
                        if m >= 2:
                            t.wait_ge(S["ev"].h, b["ev"] + m - 1)
                        pB = pb[d % 2]
                        for j in range(HCH):
                            ins = t.matmul(
                                pB[:, 0:TT],
                                _flat(w2b, (d % 2) * HCH * 128 + j * 128,
                                      128, WROW),
                                _flat(hqbf, (i % 2) * HCH * TT + j * TT,
                                      TT, QROW),
                                start=(j == 0), stop=(j == HCH - 1))
                        S["mm"].inc(ins)

            @blk.gpsimd
            def _(g):
                g.wait_ge(S["cc"].h, ncc)
                DMA(g, S["g"], row128[:], cc_out[0:1, :])
                for i in range(NT):
                    g.wait_ge(VCH.h, VC["res", NCH * (i + 1) - 1])
                    DMA(g, S["sp"], 
                        dst_fn(i),
                        bass.AP(x1b, (i % 2) * NCH * TT,
                                    [[XROW, 128], [TT, NCH], [1, TT]]))
                g.wait_ge(S["sp"].h, S["sp"].n)

    def emit_attn():
        b0_ = snap()
        with (
            nc.sbuf_tensor("win_sb", [128, NCH, D], BF16) as win_sb,
            nc.sbuf_tensor("wa_sb", [128, NCH, D], BF16) as wa_sb,
            nc.sbuf_tensor("wb_sb", [128, NCH, D], BF16) as wb_sb,
            nc.sbuf_tensor("wv_sb", [128, NCH, D], BF16) as wv_sb,
            nc.sbuf_tensor("wo_sb", [128, NCH, D], BF16) as wo_sb,
        ):
            wsrcs = [winT_r, waT_r, wbT_r, wvT_r, woT_r]
            wdsts = [win_sb, wa_sb, wb_sb, wv_sb, wo_sb]
            VC0 = {}
            with nc.sbuf_tensor("wfb", [128, 2, NCH, D], F32) as wfb, \
                    nc.Block("b6a") as blk:
                @blk.vector
                def _(v):
                    C(v.memset(rmax[:], 0.0))
                    for w in range(5):
                        v.wait_ge(S["ld"].h, b0_["ld"] + 16 * (w + 1))
                        VC0["cnv", w] = C(v.tensor_copy(
                            out=_flat(wdsts[w], 0, NCH * D, NCH * D),
                            in_=_flat(wfb, (w % 2) * NCH * D, NCH * D,
                                      2 * NCH * D)))

                @blk.sync
                def _(s):
                    for w in range(5):
                        if w >= 2:
                            s.wait_ge(VCH.h, VC0["cnv", w - 2])
                        DMA(s, S["ld"], 
                            bass.AP(wfb, (w % 2) * NCH * D,
                                        [[2 * NCH * D, 128], [D, NCH], [1, D]]),
                            wsrcs[w])

            b = snap()
            es = contextlib.ExitStack()

            def lsb(name, shape, dt):
                return es.enter_context(nc.sbuf_tensor(name, shape, dt))

            xa = lsb("xa", [128, 2, NCH, TA], F32)
            xabf = lsb("xabf", [128, 1, NCH, TA], BF16)
            sqbf = lsb("sqbf", [128, 1, NCH, TA], BF16)
            stb = lsb("stb", [128, 1, 4, TA], F32)
            rstd = lsb("rstd", [128, 1, TA], F32)
            lnf = lsb("lnf", [128, NCH, TA], F32)
            ln2bf = lsb("ln2bf", [128, 2, NCH, TA], BF16)
            sgx = lsb("sgx", [128, 2, TA], F32)
            xpbf = lsb("xpbf", [128, 2, NCH, TA], BF16)
            af32 = lsb("af32", [128, 2, TA], F32)
            bf32 = lsb("bf32", [128, 2, TA], F32)
            vf32 = lsb("vf32", [128, 2, TA], F32)
            bvf = lsb("bvf", [128, 2, TA], F32)
            hb = lsb("hb", [128, 2, NCH, TA], F32)
            sg2 = lsb("sg2", [128, 2, TA], F32)
            h22 = lsb("h22", [128, 2, TA], F32)
            cellbf = lsb("cellbf", [128, 2, NCH, TA], BF16)
            x2b = lsb("x2b", [128, 1, NCH, TA], F32)
            xbf2 = lsb("xbf2", [128, 1, NCH, TA], BF16)
            sqbf2 = lsb("sqbf2", [128, 1, NCH, TA], BF16)
            stb3 = lsb("stb3", [128, 1, 4, TA], F32)
            rstd3 = lsb("rstd3", [128, 1, TA], F32)
            ln3b = lsb("ln3b", [128, 1, NCH, TA], F32)
            with nc.Block("b6b") as blk:
                XROW = 2 * NCH * TA
                X1ROW = NCH * TA
                SROW = 4 * TA
                RROW = 2 * TA
                FROW = NCH * TA
                VC = {}

                rowlens = {id(xa): XROW, id(xabf): X1ROW, id(sqbf): X1ROW,
                           id(ln2bf): XROW, id(xpbf): XROW, id(hb): XROW,
                           id(cellbf): XROW, id(x2b): X1ROW, id(xbf2): X1ROW,
                           id(sqbf2): X1ROW, id(ln3b): X1ROW, id(lnf): FROW}

                def f3(th, k):
                    return bass.AP(th, k * NCH * TA,
                                   [[rowlens[id(th)], 128], [TA, NCH], [1, TA]])

                def ff(th, k, n=NCH * TA, extra=0):
                    return _flat(th, k * NCH * TA + extra, n, rowlens[id(th)])

                def stslot(th, slot):
                    return _flat(th, slot * TA, TA, SROW)

                def stslot_b(th, slot):
                    return bass.AP(th, slot * TA,
                                   [[SROW, 128], [0, NCH], [1, TA]])

                def rstd_bc(th):
                    return bass.AP(th, 0, [[TA, 128], [0, NCH], [1, TA]])

                def half(th, k):
                    return _flat(th, k * TA, TA, RROW)

                @blk.vector
                def _(v):
                    for i in range(NA):
                        k = i % 2
                        v.wait_ge(S["ld"].h, b["ld"] + 16 * (i + 1))
                        if i >= 1:
                            v.wait_ge(S["mm2"].h, b["mm2"] + i)
                        VC["pre", i] = C(v.tensor_copy(out=ff(xabf, 0),
                                                       in_=ff(xa, k)))
                        v.wait_ge(S["mm2"].h, b["mm2"] + i + 1)
                        C(v.tensor_scalar(stslot(stb, 0), pb[6][:, 0:TA],
                                          1.0 / D, None, OP.mult))
                        C(v.tensor_scalar(stslot(stb, 1), pb[7][:, 0:TA],
                                          1.0 / D, None, OP.mult))
                        C(v.tensor_mul(stslot(stb, 3), stslot(stb, 0),
                                       stslot(stb, 0)))
                        VC["var", i] = C(v.tensor_sub(stslot(stb, 2),
                                                      stslot(stb, 1),
                                                      stslot(stb, 3)))
                        v.wait_ge(S["sd"].h, b["sd"] + i + 1)
                        C(v.reciprocal(out=_flat(rstd, 0, TA, TA),
                                       in_=stslot(stb, 3)))
                        if i >= 2:
                            v.wait_ge(S["mi"].h, b["mi"] + NCH * (i - 1))
                        C(v.tensor_tensor(
                            out=bass.AP(lnf, 0, [[FROW, 128], [TA, NCH],
                                                 [1, TA]]),
                            in0=f3(xa, k), in1=stslot_b(stb, 0),
                            op=OP.subtract))
                        VC["qa", i] = C(v.tensor_tensor(
                            out=f3(ln2bf, k),
                            in0=bass.AP(lnf, 0, [[FROW, 128], [TA, NCH],
                                                 [1, TA]]),
                            in1=rstd_bc(rstd), op=OP.mult))
                        for e in range(NCH):
                            me = NCH * i + e
                            v.wait_ge(S["xps"].h, b["xps"] + me + 1)
                            if i >= 2 and e == 0:
                                v.wait_ge(S["pv"].h, b["pv"] + NCH * (i - 1))
                            VC["xp", me] = C(v.tensor_mul(
                                ff(xpbf, k, TA, e * TA),
                                pb[2 + e % 2][:, 0:TA], half(sgx, e % 2)))
                        for f in range(NCH):
                            mf = NCH * i + f
                            v.wait_ge(S["bee"].h, b["bee"] + mf + 1)
                            v.wait_ge(S["vee"].h, b["vee"] + mf + 1)
                            VC["bv", mf] = C(v.tensor_mul(half(bvf, f % 2),
                                                          half(bf32, f % 2),
                                                          half(vf32, f % 2)))
                            v.wait_ge(S["ale"].h, b["ale"] + mf + 1)
                            init = (0.0 if i == 0 else
                                    bass.AP(hb, (1 - k) * NCH * TA + f * TA
                                            + TA - 1, [[XROW, 128], [1, 1]]))
                            VC["scan", mf] = C(v.tensor_tensor_scan(
                                ff(hb, k, TA, f * TA), half(af32, f % 2),
                                half(bvf, f % 2), init, OP.mult, OP.add))
                        for f in range(NCH):
                            mf = NCH * i + f
                            v.wait_ge(S["sg"].h, b["sg"] + mf + 1)
                            if f == 0 and i >= 2:
                                v.wait_ge(S["mo"].h, b["mo"] + NCH * (i - 1))
                            VC["cell", mf] = C(v.tensor_mul(
                                ff(cellbf, k, TA, f * TA),
                                half(h22, f % 2), half(sg2, f % 2)))
                        for d in range(NCH):
                            md = NCH * i + d
                            v.wait_ge(S["mo"].h, b["mo"] + md + 1)
                            if d == 0 and i >= 1:
                                v.wait_ge(S["spx2"].h, b["spx2"] + 16 * i)
                                v.wait_ge(S["sq2"].h, b["sq2"] + i)
                            VC["x2", md] = C(v.tensor_add(
                                ff(x2b, 0, TA, d * TA), pb[1][:, 0:TA],
                                ff(xa, k, TA, d * TA)))
                        if i >= 1:
                            v.wait_ge(S["mm3"].h, b["mm3"] + i)
                        VC["a3", i] = C(v.tensor_copy(out=ff(xbf2, 0),
                                                      in_=ff(x2b, 0)))
                        v.wait_ge(S["mm3"].h, b["mm3"] + i + 1)
                        C(v.tensor_scalar(stslot(stb3, 0), pb[6][:, 0:TA],
                                          1.0 / D, None, OP.mult))
                        C(v.tensor_scalar(stslot(stb3, 1), pb[7][:, 0:TA],
                                          1.0 / D, None, OP.mult))
                        C(v.tensor_mul(stslot(stb3, 3), stslot(stb3, 0),
                                       stslot(stb3, 0)))
                        VC["var3", i] = C(v.tensor_sub(stslot(stb3, 2),
                                                       stslot(stb3, 1),
                                                       stslot(stb3, 3)))
                        v.wait_ge(S["sd3"].h, b["sd3"] + i + 1)
                        C(v.reciprocal(out=_flat(rstd3, 0, TA, TA),
                                       in_=stslot(stb3, 3)))
                        if i >= 1:
                            v.wait_ge(S["spln"].h, b["spln"] + 16 * i)
                        C(v.tensor_tensor(
                            out=bass.AP(lnf, 0, [[FROW, 128], [TA, NCH],
                                                 [1, TA]]),
                            in0=f3(x2b, 0), in1=stslot_b(stb3, 0),
                            op=OP.subtract))
                        C(v.tensor_tensor(
                            out=f3(ln3b, 0),
                            in0=bass.AP(lnf, 0, [[FROW, 128], [TA, NCH],
                                                 [1, TA]]),
                            in1=rstd_bc(rstd3), op=OP.mult))
                        C(v.reduce_max(out=tmax[:], in_=ff(ln3b, 0), axis=AX.X,
                                       apply_absolute_value=True))
                        VC["ln3", i] = C(v.tensor_tensor(
                            out=rmax[:], in0=rmax[:], in1=tmax[:], op=OP.max))

                @blk.sync
                def _(s):
                    for i in range(NA):
                        if i >= 2:
                            s.wait_ge(VCH.h, VC["x2", NCH * (i - 1) - 1])
                            s.wait_ge(S["sq"].h, b["sq"] + i - 1)
                        src = x1_d[:, :, i * TA:(i + 1) * TA].transpose([1, 0, 2])
                        DMA(s, S["ld"], f3(xa, i % 2), src)

                @blk.scalar
                def _(sc):
                    for i in range(NA):
                        k = i % 2
                        sc.wait_ge(S["ld"].h, b["ld"] + 16 * (i + 1))
                        if i >= 1:
                            sc.wait_ge(S["mm2"].h, b["mm2"] + i)
                        S["sq"].inc(sc.activation(out=ff(sqbf, 0),
                                                  in_=ff(xa, k),
                                                  func=AF.Square))
                        sc.wait_ge(VCH.h, VC["var", i])
                        S["sd"].inc(sc.activation(out=stslot(stb, 3),
                                                  in_=stslot(stb, 2),
                                                  func=AF.Sqrt, bias=EPS_A))
                        for e in range(NCH):
                            me = NCH * i + e
                            sc.wait_ge(S["mi"].h, b["mi"] + me + 1)
                            if me >= 2:
                                sc.wait_ge(VCH.h, VC["xp", me - 2])
                            S["xps"].inc(sc.activation(
                                out=half(sgx, e % 2),
                                in_=pb[2 + e % 2][:, 0:TA], func=AF.Sigmoid))
                        for f in range(NCH):
                            mf = NCH * i + f
                            sc.wait_ge(S["pv"].h, b["pv"] + mf + 1)
                            if mf >= 2:
                                sc.wait_ge(VCH.h, VC["scan", mf - 2])
                                sc.wait_ge(VCH.h, VC["bv", mf - 2])
                            S["ale"].inc(sc.activation(
                                out=half(af32, f % 2), in_=pb[4][:, 0:TA],
                                func=AF.Sigmoid, bias=bap_s[:, f:f + 1]))
                            S["bee"].inc(sc.activation(
                                out=half(bf32, f % 2), in_=pb[5][:, 0:TA],
                                func=AF.Sigmoid, bias=bbp_s[:, f:f + 1]))
                            S["vee"].inc(sc.activation(
                                out=half(vf32, f % 2), in_=pb[0][:, 0:TA],
                                func=AF.Tanh, bias=bvp_s[:, f:f + 1]))
                        for f in range(NCH):
                            mf = NCH * i + f
                            sc.wait_ge(VCH.h, VC["scan", mf])
                            if mf >= 2:
                                sc.wait_ge(VCH.h, VC["cell", mf - 2])
                            sc.activation(out=half(sg2, f % 2),
                                          in_=ff(hb, k, TA, f * TA),
                                          func=AF.Sigmoid)
                            S["sg"].inc(sc.activation(out=half(h22, f % 2),
                                                      in_=ff(hb, k, TA, f * TA),
                                                      func=AF.Square))
                        sc.wait_ge(VCH.h, VC["x2", NCH * (i + 1) - 1])
                        if i >= 1:
                            sc.wait_ge(S["mm3"].h, b["mm3"] + i)
                        S["sq2"].inc(sc.activation(out=ff(sqbf2, 0),
                                                   in_=ff(x2b, 0),
                                                   func=AF.Square))
                        sc.wait_ge(VCH.h, VC["var3", i])
                        S["sd3"].inc(sc.activation(out=stslot(stb3, 3),
                                                   in_=stslot(stb3, 2),
                                                   func=AF.Sqrt, bias=EPS_A))

                @blk.tensor
                def _(t):
                    for i in range(NA):
                        k = i % 2
                        t.wait_ge(VCH.h, VC["pre", i])
                        t.wait_ge(S["sq"].h, b["sq"] + i + 1)
                        if i >= 1:
                            t.wait_ge(VCH.h, VC["var", i - 1])
                            t.wait_ge(VCH.h, VC["var3", i - 1])
                        for c in range(NCH):
                            t.matmul(pb[6][:, 0:TA], ones_bf[:],
                                     ff(xabf, 0, TA, c * TA),
                                     start=(c == 0), stop=(c == NCH - 1))
                        for c in range(NCH):
                            ins = t.matmul(pb[7][:, 0:TA], ones_bf[:],
                                           ff(sqbf, 0, TA, c * TA),
                                           start=(c == 0), stop=(c == NCH - 1))
                        S["mm2"].inc(ins)
                        t.wait_ge(VCH.h, VC["qa", i])
                        for e in range(NCH):
                            me = NCH * i + e
                            if me >= 2:
                                t.wait_ge(VCH.h, VC["xp", me - 2])
                            for c in range(NCH):
                                ins = t.matmul(
                                    pb[2 + e % 2][:, 0:TA],
                                    bass.AP(win_sb, c * D + e * 128,
                                            [[NCH * D, 128], [1, 128]]),
                                    ff(ln2bf, k, TA, c * TA),
                                    start=(c == 0), stop=(c == NCH - 1))
                            S["mi"].inc(ins)
                        t.wait_ge(VCH.h, VC["xp", NCH * (i + 1) - 1])
                        for f in range(NCH):
                            mf = NCH * i + f
                            if mf >= 1:
                                t.wait_ge(S["ale"].h, b["ale"] + mf)
                                t.wait_ge(S["bee"].h, b["bee"] + mf)
                                t.wait_ge(S["vee"].h, b["vee"] + mf)
                            for (bank, wsb) in ((4, wa_sb), (5, wb_sb),
                                                (0, wv_sb)):
                                for c in range(NCH):
                                    ins = t.matmul(
                                        pb[bank][:, 0:TA],
                                        bass.AP(wsb, c * D + f * 128,
                                                [[NCH * D, 128], [1, 128]]),
                                        ff(xpbf, k, TA, c * TA),
                                        start=(c == 0), stop=(c == NCH - 1))
                            S["pv"].inc(ins)
                        t.wait_ge(VCH.h, VC["cell", NCH * (i + 1) - 1])
                        for d in range(NCH):
                            md = NCH * i + d
                            if md >= 1:
                                t.wait_ge(VCH.h, VC["x2", md - 1])
                            for c in range(NCH):
                                ins = t.matmul(
                                    pb[1][:, 0:TA],
                                    bass.AP(wo_sb, c * D + d * 128,
                                            [[NCH * D, 128], [1, 128]]),
                                    ff(cellbf, k, TA, c * TA),
                                    start=(c == 0), stop=(c == NCH - 1))
                            S["mo"].inc(ins)
                        t.wait_ge(VCH.h, VC["a3", i])
                        t.wait_ge(S["sq2"].h, b["sq2"] + i + 1)
                        t.wait_ge(VCH.h, VC["var", i])
                        if i >= 1:
                            t.wait_ge(VCH.h, VC["var3", i - 1])
                        for c in range(NCH):
                            t.matmul(pb[6][:, 0:TA], ones_bf[:],
                                     ff(xbf2, 0, TA, c * TA),
                                     start=(c == 0), stop=(c == NCH - 1))
                        for c in range(NCH):
                            ins = t.matmul(pb[7][:, 0:TA], ones_bf[:],
                                           ff(sqbf2, 0, TA, c * TA),
                                           start=(c == 0), stop=(c == NCH - 1))
                        S["mm3"].inc(ins)

                @blk.gpsimd
                def _(g):
                    for i in range(NA):
                        g.wait_ge(VCH.h, VC["x2", NCH * (i + 1) - 1])
                        dst = x2_d[:, :, i * TA:(i + 1) * TA].transpose([1, 0, 2])
                        DMA(g, S["spx2"], dst, f3(x2b, 0))
                        g.wait_ge(VCH.h, VC["ln3", i])
                        dst2 = ln_d[:, :, i * TA:(i + 1) * TA].transpose([1, 0, 2])
                        DMA(g, S["spln"], dst2, f3(ln3b, 0))
                    DMA(g, S["spln"], cc_in[0:1, :].transpose([1, 0]),
                                              rmax[:, 0:1])
                    g.wait_ge(S["spx2"].h, S["spx2"].n)
                    g.wait_ge(S["spln"].h, S["spln"].n)
                    S["cc"].inc(g.collective_compute(
                        "AllReduce", OP.max, replica_groups=[list(range(8))],
                        ins=[cc_in], outs=[cc_out]))
            es.close()

    emit_passB(ncc=1, col=0)
    emit_passC(ncc=2, col=3,
               res_src_fn=lambda i: xT_r[:, :, i * TT:(i + 1) * TT],
               dst_fn=lambda i: x1_d[:, :, i * TT:(i + 1) * TT].transpose([1, 0, 2]))
    emit_attn()
    emit_passB(ncc=3, col=6)
    emit_passC(ncc=4, col=9,
               res_src_fn=lambda i: x2_d[:, :, i * TT:(i + 1) * TT].transpose([1, 0, 2]),
               dst_fn=lambda i: outT_r[:, :, i * TT:(i + 1) * TT])

    nc.finalize()
    st.close()
    return nc


def kernel(L0_attn_x, L0_attn_W_in, L0_attn_W_alpha, L0_attn_b_alpha,
           L0_attn_W_beta, L0_attn_b_beta, L0_attn_W_v, L0_attn_b_v,
           L0_attn_W_out, L0_ffn_w1, L0_ffn_b1, L0_ffn_w2, L0_ffn_b2):
    f32 = np.float32
    x = np.asarray(L0_attn_x, dtype=f32)
    w1T = np.ascontiguousarray(np.asarray(L0_ffn_w1, f32).T)
    w2T = np.ascontiguousarray(np.asarray(L0_ffn_w2, f32).T)
    winT = np.ascontiguousarray(np.asarray(L0_attn_W_in, f32).T)
    waT = np.ascontiguousarray(np.asarray(L0_attn_W_alpha, f32).T)
    wbT = np.ascontiguousarray(np.asarray(L0_attn_W_beta, f32).T)
    wvT = np.ascontiguousarray(np.asarray(L0_attn_W_v, f32).T)
    woT = np.ascontiguousarray(np.asarray(L0_attn_W_out, f32).T)
    b1pm = np.ascontiguousarray(np.asarray(L0_ffn_b1, f32).reshape(HCH, 128).T)
    b2pm = np.ascontiguousarray(np.asarray(L0_ffn_b2, f32).reshape(NCH, 128).T)
    bapm = np.ascontiguousarray(np.asarray(L0_attn_b_alpha, f32).reshape(NCH, 128).T)
    bbpm = np.ascontiguousarray(np.asarray(L0_attn_b_beta, f32).reshape(NCH, 128).T)
    bvpm = np.ascontiguousarray(np.asarray(L0_attn_b_v, f32).reshape(NCH, 128).T)

    nc = build_program()
    in_maps = []
    for bb in range(8):
        in_maps.append({
            "xT": np.ascontiguousarray(x[bb].T),
            "w1T": w1T, "w2T": w2T,
            "winT": winT, "waT": waT, "wbT": wbT, "wvT": wvT, "woT": woT,
            "b1p": b1pm, "b2p": b2pm,
            "bap": bapm, "bbp": bbpm, "bvp": bvpm,
        })
    res = run_bass_kernel_spmd(nc, in_maps, core_ids=list(range(8)))
    out = np.stack([np.asarray(res.results[bb]["outT"]).T for bb in range(8)])
    return out.astype(np.float32)
